# version 16
# baseline (speedup 1.0000x reference)
import math
from collections import defaultdict
from contextlib import ExitStack

import numpy as np
import ml_dtypes

import concourse.bass as bass
import concourse.mybir as mybir
from concourse.bass_utils import run_bass_kernel_spmd

F32 = mybir.dt.float32
BF16 = mybir.dt.bfloat16
AF = mybir.ActivationFunctionType
ALU = mybir.AluOpType

B = 16
BPC = 2
S = 2048
D = 1024
NCH = 8
NH = 16
HD = 64
NB = 32
MAXD = 128
E = 16
EPS = 1e-5
GSCALE = 0.5 / 1.702
NKT = 16
BSW = 3968

ENGINES = ("sync", "tensor", "vector", "scalar", "gpsimd")
DBG = False


class Sched:
    def __init__(self):
        self.ops = []
        self.cnt = defaultdict(int)
        self.pending = {}

    def add(self, eng, fn, waits=(), sem=None, inc=1):
        w = list(self.pending.pop(eng, [])) + [(s, v) for (s, v) in waits if v > 0]
        incs = []
        if sem is not None:
            self.cnt[sem] += inc
            incs.append((sem, inc))
        self.ops.append((eng, fn, w, incs))
        return self.cnt[sem] if sem is not None else None

    def dma(self, eng, fn, waits=(), sem="dX"):
        return self.add(eng, fn, waits=waits, sem=sem, inc=16)

    def barrier(self):
        snap = [(s, self.cnt[s]) for s in list(self.cnt)]
        for e in ENGINES:
            self.pending[e] = snap

    def simulate(self):
        queues = {e: [] for e in ENGINES}
        for op in self.ops:
            queues[op[0]].append(op)
        pos = {e: 0 for e in ENGINES}
        cnt = defaultdict(int)
        progress = True
        while progress:
            progress = False
            for e in ENGINES:
                while pos[e] < len(queues[e]):
                    _, _, waits, incs = queues[e][pos[e]]
                    if all(cnt[s] >= v for (s, v) in waits):
                        for (s, n) in incs:
                            cnt[s] += n
                        pos[e] += 1
                        progress = True
                    else:
                        break
        stuck = {e: (pos[e], len(queues[e])) for e in ENGINES if pos[e] < len(queues[e])}
        if stuck:
            detail = {}
            for e in stuck:
                _, _, waits, _ = queues[e][pos[e]]
                detail[e] = [(s, v, cnt[s]) for (s, v) in waits]
            raise RuntimeError(f"schedule deadlock: stuck={stuck} waits={detail}")


class PsumSlots:
    def __init__(self, slots):
        self.slots = slots
        self.last = [None for _ in slots]
        self.rr = 0

    def acquire(self):
        i = self.rr % len(self.slots)
        self.rr += 1
        w = [self.last[i]] if self.last[i] else []
        return i, self.slots[i], w

    def release(self, i, sem, token):
        self.last[i] = (sem, token)


def build_nc():
    nc = bass.Bass("TRN2", target_bir_lowering=False)

    XIN = nc.dram_tensor("XIN", [BPC, D, S], F32, kind="ExternalInput").ap()
    OUT = nc.dram_tensor("OUT", [BPC, D, S], F32, kind="ExternalOutput").ap()
    X1 = nc.dram_tensor("X1", [BPC, D, S], F32, kind="Internal").ap()
    X2 = nc.dram_tensor("X2", [BPC, D, S], F32, kind="Internal").ap()
    W1P = nc.dram_tensor("W1P", [E, 8, D, 256], F32, kind="ExternalInput").ap()
    W2P = nc.dram_tensor("W2P", [E, 8, D, 128], F32, kind="ExternalInput").ap()
    B1H = nc.dram_tensor("B1H", [E, 128, 16], F32, kind="ExternalInput").ap()
    B2H = nc.dram_tensor("B2H", [E, D], F32, kind="ExternalInput").ap()
    WQKV = nc.dram_tensor("WQKV", [D, 3 * D], F32, kind="ExternalInput").ap()
    QKP = nc.dram_tensor("QKP", [D, 4096], F32, kind="ExternalInput").ap()
    BQH = nc.dram_tensor("BQH", [128, 32], F32, kind="ExternalInput").ap()
    BVR = nc.dram_tensor("BVR", [1, D], F32, kind="ExternalInput").ap()
    WD = nc.dram_tensor("WD", [D, D], F32, kind="ExternalInput").ap()
    BDH = nc.dram_tensor("BDH", [128, 8], F32, kind="ExternalInput").ap()
    NSH = nc.dram_tensor("NSH", [128, 8], F32, kind="ExternalInput").ap()
    WGH = nc.dram_tensor("WGH", [8, 128, 16], F32, kind="ExternalInput").ap()
    BGH = nc.dram_tensor("BGH", [1, 16], F32, kind="ExternalInput").ap()
    BSH = nc.dram_tensor("BSH", [NH, 128, BSW], F32, kind="ExternalInput").ap()
    IDE = nc.dram_tensor("IDE", [128, 128], F32, kind="ExternalInput").ap()
    if DBG:
        DXN = nc.dram_tensor("DXN", [128, 8 * S], BF16, kind="ExternalOutput").ap()
        DGFM = nc.dram_tensor("DGFM", [16, S], BF16, kind="ExternalOutput").ap()
        DT = nc.dram_tensor("DT", [128, 8 * S], F32, kind="ExternalOutput").ap()
        DROW = nc.dram_tensor("DROW", [1, 2 * S], F32, kind="ExternalOutput").ap()
        DRT = nc.dram_tensor("DRT", [128, 160], F32, kind="ExternalOutput").ap()
        DSP = nc.dram_tensor("DSP", [128, 16384], BF16, kind="ExternalOutput").ap()
        DXL = nc.dram_tensor("DXL", [128, 8 * S], BF16, kind="ExternalOutput").ap()
        DVT = nc.dram_tensor("DVT", [128, 16640], BF16, kind="ExternalOutput").ap()
        DQ = nc.dram_tensor("DQ", [128, S], BF16, kind="ExternalOutput").ap()
        DK = nc.dram_tensor("DK", [128, S], BF16, kind="ExternalOutput").ap()
        DCTX = nc.dram_tensor("DCTX", [128, 8 * S], BF16, kind="ExternalOutput").ap()
        DEB = nc.dram_tensor("DEB", [128, BSW], F32, kind="ExternalOutput").ap()
        DNR = nc.dram_tensor("DNR", [4, S], F32, kind="ExternalOutput").ap()
        DES = nc.dram_tensor("DES", [128, S], BF16, kind="ExternalOutput").ap()

    ctx = ExitStack()
    with ctx:
        sb = lambda n, sh, dt: ctx.enter_context(nc.sbuf_tensor(n, sh, dt))
        A = sb("A", [128, 8 * S], BF16)
        Bv = sb("Bv", [128, 16640], BF16)
        C = sb("C", [128, 8 * S], F32)
        W = sb("W", [128, 15360], BF16)
        ATT = sb("ATT", [128, 4096], F32)
        R1 = sb("R1", [1, S], F32)
        R2 = sb("R2", [1, S], F32)
        rt_sb = sb("rt", [128, 160], F32)
        bq_sb = sb("bq", [128, 32], F32)
        bd_sb = sb("bd", [128, 8], F32)
        ns_sb = sb("ns", [128, 8], F32)
        wgs_sb = sb("wgs", [128, 8 * 16], F32)
        b2v_sb = sb("b2v", [16, D], BF16)
        bg_sb = sb("bg", [1, 16], F32)
        bgbc_sb = sb("bgbc", [128, 16], F32)
        negc_sb = sb("negc", [1, 16], F32)
        gfm_sb = sb("gfm", [16, S], BF16)
        gsc_sb = sb("gsc", [128, 64], F32)
        gfb_sb = sb("gfb", [128, 16], BF16)
        b1_sb = sb("b1", [128, 2 * 16], F32)
        b1p_sb = sb("b1p", [128, 2 * 8], F32)
        bvt_sb = sb("bvt", [128, D], BF16)
        ones_col = sb("onesc", [128, 1], F32)
        ones_rf = sb("onesrf", [1, 128], F32)
        ones_rb = sb("onesrb", [1, 128], BF16)
        row0_sb = sb("row0", [128, 128], BF16)
        bcsrc_sb = sb("bcsrc", [128, 2048], BF16)
        ident_sb = sb("ident", [128, 128], BF16)
        acol_sb = sb("acol", [128, 2], F32)
        gfx_sb = sb("gfx", [128, 16], F32)
        tmpx_sb = sb("tmpx", [128, 16], F32)

        psA = ctx.enter_context(nc.psum_tensor("psA", [128, 2048], F32))
        psB = ctx.enter_context(nc.psum_tensor("psB", [128, 2048], F32))

        sems = {}
        for s in ("dW", "dX", "dS", "dO", "pe", "dv", "ac", "gp"):
            sems[s] = ctx.enter_context(nc.semaphore(s))
        block = ctx.enter_context(nc.Block())

        sch = Sched()
        slots = PsumSlots([psA, psB])
        w1_slot_tok = {}
        w2_slot_tok = {}

        Cb = C[:, :].bitcast(BF16)
        Wf = W[:, :].bitcast(F32)
        psA_bf = psA[:, :].bitcast(BF16)
        psB_bf = psB[:, :].bitcast(BF16)
        ps_bf = {id(psA): psA_bf, id(psB): psB_bf}

        def Af(c):
            return A[:, c * S:(c + 1) * S]

        def Cf(c):
            return C[:, c * S:(c + 1) * S]

        def w1slot(i):
            return W[:, (i % 4) * 2048:(i % 4) * 2048 + 2048]

        def w2slot(i):
            return W[:, 8192 + (i % 3) * 1024:8192 + (i % 3) * 1024 + 1024]

        def T1(i):
            return Wf[:, 5632 + (i % 2) * 512:5632 + (i % 2) * 512 + 512]

        T2 = W[:, 13312:15360]

        def wv_sl(dc):
            return W[:, dc * 1024:(dc + 1) * 1024]

        bsh_stage = Wf[:, 0:BSW]
        q_tile = W[:, 8192:10240]
        k_tile = W[:, 10240:12288]

        def qkw(i):
            return W[:, 12288 + (i % 2) * 1024:12288 + (i % 2) * 1024 + 1024]

        def wdw(i):
            return W[:, 12288 + (i % 2) * 1024:12288 + (i % 2) * 1024 + 1024]

        def ATTh(i):
            return ATT[:, (i % 2) * 2048:(i % 2) * 2048 + 2048]

        M_t = ATT[:, 0:2048]
        Ab_t = ATT[:, 2048:4096]
        ebias_t = ATT[:, 0:BSW]
        ATTb = ATT[:, :].bitcast(BF16)
        ebias_b = ATTb[:, 0:BSW]

        def gtile(i):
            return ATT[:, (i % 2) * 2048:(i % 2) * 2048 + 2048]

        def xstg(i):
            return C[:, 2048 + (i % 2) * 2048:2048 + (i % 2) * 2048 + 2048]

        def exptmp(i):
            return C[:, 4096 + (i % 2) * 2048:4096 + (i % 2) * 2048 + 2048]

        def exptmp_b(i):
            return Cb[:, 8192 + (i % 2) * 2048:8192 + (i % 2) * 2048 + 2048]

        def expS(i):
            return Cb[:, (i % 2) * 2048:(i % 2) * 2048 + 2048]

        def ctxf(c):
            return Cb[:, 16384 + c * S:16384 + (c + 1) * S]

        def spr(pr):
            return Bv[:, pr * 2048:(pr + 1) * 2048]

        def vsl(tc, h):
            return Bv[:, tc * 1040 + h * 65:tc * 1040 + h * 65 + 65]

        m_row = R1[0:1, 0:S]
        A_row = R2[0:1, 0:S]
        RT = {"sum": 0, "sq": 16, "m": 32, "v": 48, "r1": 64, "r": 80,
              "u": 96, "rr": 112, "A": 128, "hr": 144}

        def rt(k):
            return rt_sb[:, RT[k]:RT[k] + 16]

        grow_sb = R2[0:1, 0:1024].bitcast(BF16)
        rrow_sb = R1[0:1, 0:S]

        war = {}

        def ww(key):
            return [war[key]] if key in war else []

        def ws(key, sem, tokv):
            war[key] = (sem, tokv)

        def tok(sem):
            return sch.cnt[sem]

        def dvw():
            return [("dv", sch.cnt["dv"])]

        sch.add("vector", lambda e: e.memset(ones_col[:, :], 1.0), sem="dv")
        sch.add("vector", lambda e: e.memset(ones_rf[:, :], 1.0), sem="dv")
        sch.add("vector", lambda e: e.memset(ones_rb[:, :], 1.0), sem="dv")
        sch.add("vector", lambda e: e.memset(row0_sb[:, :], 0.0), sem="dv")
        sch.add("vector", lambda e: e.memset(bcsrc_sb[:, :], 0.0), sem="dv")
        sch.add("vector", lambda e: e.memset(row0_sb[0:1, :], 1.0),
                waits=[("dv", sch.cnt["dv"])], sem="dv")
        tok_ones = tok("dv")
        sch.dma("sync", lambda e: e.dma_start(out=bq_sb[:, :], in_=BQH))
        sch.dma("sync", lambda e: e.dma_start(out=bd_sb[:, :], in_=BDH))
        sch.dma("sync", lambda e: e.dma_start(out=ns_sb[:, :], in_=NSH))
        sch.dma("sync", lambda e: e.dma_start(
            out=wgs_sb[:, :].rearrange("p (k e) -> p k e", k=8),
            in_=WGH.rearrange("k p e -> p k e")))
        sch.dma("sync", lambda e: e.dma_start(out=bg_sb[:, :], in_=BGH))
        sch.dma("gpsimd", lambda e: e.dma_start(out=ident_sb[:, :], in_=IDE), sem="dW")
        sch.dma("gpsimd", lambda e: e.dma_start(out=b2v_sb[:, :], in_=B2H), sem="dW")
        t_dx0 = tok("dX")
        t_dw0 = tok("dW")

        for k in range(8):
            sch.add("vector", lambda e, k=k: e.tensor_scalar(
                out=wgs_sb[:, k * 16:(k + 1) * 16], in0=wgs_sb[:, k * 16:(k + 1) * 16],
                scalar1=ns_sb[:, k:k + 1], scalar2=None, op0=ALU.mult),
                waits=[("dX", t_dx0)], sem="dv")
        t_wgs = tok("dv")
        sch.add("vector", lambda e: e.tensor_scalar(
            out=b2v_sb[:, :], in0=b2v_sb[:, :], scalar1=1.702, scalar2=None, op0=ALU.mult),
            waits=[("dW", tok("dW"))], sem="dv")
        t_b2v = tok("dv")

        t_bvr = sch.dma("sync", lambda e: e.dma_start(
            out=R1[0:1, 0:1024], in_=BVR))
        si, sl, sw = slots.acquire()
        sch.add("tensor", lambda e, sl=sl: e.matmul(
            sl[:, 0:16], ones_rf[:, :], bg_sb[0:1, :], start=True, stop=True),
            waits=sw + [("dX", t_dx0), ("dv", tok_ones)], sem=None)
        for tb in range(2):
            sch.add("tensor", lambda e, sl=sl, tb=tb: e.matmul(
                sl[:, 512 + tb * 512:512 + (tb + 1) * 512], ones_rf[:, :],
                R1[0:1, tb * 512:(tb + 1) * 512], start=True, stop=True),
                waits=[("dX", t_bvr)], sem="pe" if tb == 1 else None)
        t_pe0 = tok("pe")
        sch.add("vector", lambda e, sl=sl: e.tensor_copy(out=bgbc_sb[:, :], in_=sl[:, 0:16]),
                waits=[("pe", t_pe0)], sem="dv")
        sch.add("vector", lambda e, sl=sl: e.tensor_copy(
            out=bvt_sb[:, :], in_=sl[:, 512:512 + 1024]), sem="dv")
        slots.release(si, "dv", tok("dv"))
        si, sl, sw = slots.acquire()
        for k in range(8):
            sch.add("tensor", lambda e, sl=sl, k=k: e.matmul(
                sl[0:1, 0:16], ones_col[:, :], wgs_sb[:, k * 16:(k + 1) * 16],
                start=(k == 0), stop=(k == 7)),
                waits=(sw + [("dv", t_wgs)] if k == 0 else []),
                sem="pe" if k == 7 else None)
        t_pe1 = tok("pe")
        sch.add("vector", lambda e, sl=sl: e.tensor_scalar(
            out=negc_sb[:, :], in0=sl[0:1, 0:16], scalar1=-1.0, scalar2=None, op0=ALU.mult),
            waits=[("pe", t_pe1)], sem="dv")
        slots.release(si, "dv", tok("dv"))

        def ln_prologue(src, bi, ffn_mode):
            x_toks = []
            for c in range(NCH):
                t = sch.dma("sync", lambda e, c=c: e.dma_start(
                    out=Cf(c), in_=src[bi, c * 128:(c + 1) * 128, :]),
                    waits=ww(("C", c)))
                x_toks.append(t)
            siA, slA, swA = slots.acquire()
            siB, slB, swB = slots.acquire()
            for c in range(NCH):
                half = c % 2
                sqv = ATTh(half)
                sch.add("scalar", lambda e, c=c, sqv=sqv: e.activation(
                    out=sqv, in_=Cf(c), func=AF.Square),
                    waits=[("dX", x_toks[c])] + ww(("ATTh", half)), sem="ac")
                t_sq = tok("ac")
                for tb in range(4):
                    sch.add("tensor", lambda e, c=c, tb=tb, slA=slA: e.matmul(
                        slA[0:1, tb * 512:(tb + 1) * 512], ones_col[:, :],
                        Cf(c)[:, tb * 512:(tb + 1) * 512],
                        start=(c == 0), stop=(c == NCH - 1)),
                        waits=[("dX", x_toks[c])] + (swA if (c == 0 and tb == 0) else []),
                        sem=None)
                    sch.add("tensor", lambda e, c=c, tb=tb, slB=slB, sqv=sqv: e.matmul(
                        slB[0:1, tb * 512:(tb + 1) * 512], ones_col[:, :],
                        sqv[:, tb * 512:(tb + 1) * 512],
                        start=(c == 0), stop=(c == NCH - 1)),
                        waits=[("ac", t_sq)] + (swB if (c == 0 and tb == 0) else []),
                        sem="pe" if tb == 3 else None)
                ws(("ATTh", half), "pe", tok("pe"))
            t_stats = tok("pe")
            sch.add("vector", lambda e, slA=slA: e.tensor_copy(out=m_row, in_=slA[0:1, :]),
                    waits=[("pe", t_stats), ("dS", tok("dS"))] + ww(("mrow",)), sem="dv")
            sch.add("vector", lambda e, slB=slB: e.tensor_copy(out=A_row, in_=slB[0:1, :]),
                    waits=ww(("Arow",)), sem="dv")
            t_ev = tok("dv")
            slots.release(siA, "dv", t_ev)
            slots.release(siB, "dv", t_ev)
            t_d1 = sch.dma("scalar", lambda e: e.dma_start(
                out=rt_sb[:, 0:16], in_=m_row),
                waits=[("dv", t_ev)] + ww(("rt",)), sem="dS")
            t_d2 = sch.dma("scalar", lambda e: e.dma_start(
                out=rt_sb[:, 16:32], in_=A_row), waits=[("dv", t_ev)], sem="dS")
            sch.add("vector", lambda e: e.tensor_scalar(
                out=rt("m"), in0=rt("sum"), scalar1=1.0 / D, scalar2=None, op0=ALU.mult),
                waits=[("dS", t_d1)], sem="dv")
            sch.add("vector", lambda e: e.tensor_tensor(
                out=rt("v"), in0=rt("m"), in1=rt("m"), op=ALU.mult),
                waits=dvw(), sem="dv")
            sch.add("vector", lambda e: e.scalar_tensor_tensor(
                out=rt("v"), in0=rt("sq"), scalar=1.0 / D, in1=rt("v"),
                op0=ALU.mult, op1=ALU.subtract),
                waits=[("dS", t_d2)] + dvw(), sem="dv")
            sch.add("vector", lambda e: e.tensor_scalar(
                out=rt("r1"), in0=rt("v"), scalar1=EPS, scalar2=None, op0=ALU.add),
                waits=dvw(), sem="dv")
            sch.add("vector", lambda e: e.reciprocal(out=rt("r1"), in_=rt("r1")),
                    waits=dvw(), sem="dv")
            t_r1 = tok("dv")
            sch.add("scalar", lambda e: e.activation(out=rt("r"), in_=rt("r1"), func=AF.Sqrt),
                    waits=[("dv", t_r1)], sem="ac")
            t_rA = tok("ac")
            acol_src = "r"
            if ffn_mode:
                sch.add("vector", lambda e: e.tensor_scalar(
                    out=rt("hr"), in0=rt("r"), scalar1=0.5, scalar2=None, op0=ALU.mult),
                    waits=[("ac", t_rA)] + dvw(), sem="dv")
                sch.add("vector", lambda e: e.tensor_tensor(
                    out=rt("u"), in0=rt("v"), in1=rt("r1"), op=ALU.mult),
                    waits=dvw(), sem="dv")
                sch.add("vector", lambda e: e.tensor_scalar(
                    out=rt("u"), in0=rt("u"), scalar1=EPS, scalar2=None, op0=ALU.add),
                    waits=dvw(), sem="dv")
                sch.add("vector", lambda e: e.reciprocal(out=rt("u"), in_=rt("u")),
                        waits=dvw(), sem="dv")
                t_u = tok("dv")
                sch.add("scalar", lambda e: e.activation(
                    out=rt("rr"), in_=rt("u"), func=AF.Sqrt),
                    waits=[("dv", t_u)], sem="ac")
                t_rr = tok("ac")
                sch.add("vector", lambda e: e.tensor_tensor(
                    out=rt("A"), in0=rt("r"), in1=rt("rr"), op=ALU.mult),
                    waits=[("ac", max(t_rA, t_rr))], sem="dv")
                acol_src = "A"
            else:
                sch.add("vector", lambda e: e.tensor_copy(out=rt("A"), in_=rt("r")),
                        waits=[("ac", t_rA)], sem="dv")
            t_At = tok("dv")
            t_d3 = sch.dma("scalar", lambda e: e.dma_start(
                out=m_row, in_=rt("m")), waits=[("dv", t_At)], sem="dS")
            t_d4 = sch.dma("scalar", lambda e: e.dma_start(
                out=A_row, in_=rt("A")), waits=[("dv", t_At)], sem="dS")
            t_A = tok("dS")
            ws(("rt",), "dS", t_A)
            return x_toks, t_A

        def build_MA_tiles(t_A):
            siA, slA, swA = slots.acquire()
            siB, slB, swB = slots.acquire()
            for tb in range(4):
                sch.add("tensor", lambda e, tb=tb, slA=slA: e.matmul(
                    slA[:, tb * 512:(tb + 1) * 512], ones_rf[:, :],
                    m_row[0:1, tb * 512:(tb + 1) * 512], start=True, stop=True),
                    waits=(swA + [("dS", t_A)] if tb == 0 else []), sem=None)
                sch.add("tensor", lambda e, tb=tb, slB=slB: e.matmul(
                    slB[:, tb * 512:(tb + 1) * 512], ones_rf[:, :],
                    A_row[0:1, tb * 512:(tb + 1) * 512], start=True, stop=True),
                    waits=(swB if tb == 0 else []), sem="pe" if tb == 3 else None)
            t_bc = tok("pe")
            sch.add("vector", lambda e, slA=slA: e.tensor_copy(out=M_t, in_=slA[:, :]),
                    waits=[("pe", t_bc)] + ww(("ATTh", 0)), sem="dv")
            sch.add("vector", lambda e, slB=slB: e.tensor_copy(out=Ab_t, in_=slB[:, :]),
                    waits=ww(("ATTh", 1)), sem="dv")
            t_MA = tok("dv")
            slots.release(siA, "dv", t_MA)
            slots.release(siB, "dv", t_MA)
            ws(("ATTh", 0), "dv", t_MA)
            ws(("ATTh", 1), "dv", t_MA)
            ws(("mrow",), "pe", t_bc)
            ws(("Arow",), "pe", t_bc)
            return t_MA

        def affine_xn(t_MA, extra_pe_wait, ffn_mode):
            for c in range(NCH):
                w = [("dv", t_MA)] + ([("pe", extra_pe_wait)] if extra_pe_wait else [])
                sch.add("vector", lambda e, c=c: e.tensor_tensor(
                    out=Cf(c), in0=Cf(c), in1=M_t, op=ALU.subtract),
                    waits=w + ww(("A", c)), sem="dv")
                if ffn_mode:
                    sch.add("vector", lambda e, c=c: e.scalar_tensor_tensor(
                        out=Af(c), in0=Cf(c), scalar=ns_sb[:, c:c + 1], in1=Ab_t,
                        op0=ALU.mult, op1=ALU.mult), sem="dv")
                else:
                    sch.add("vector", lambda e, c=c: e.tensor_tensor(
                        out=Af(c), in0=Cf(c), in1=Ab_t, op=ALU.mult), sem="dv")
            t_xn = tok("dv")
            ws(("ATTh", 0), "dv", t_xn)
            ws(("ATTh", 1), "dv", t_xn)
            return t_xn

        def ffn_call(src, dst, bi, out_sem, dbg=False):
            x_toks, t_A = ln_prologue(src, bi, ffn_mode=True)

            lg = gsc_sb[:, 0:16]
            m8 = gsc_sb[:, 16:24]
            e4 = gsc_sb[:, 24:28]
            e4x = gsc_sb[:, 52:56]
            scs = gsc_sb[:, 28:29]
            rcs = gsc_sb[:, 30:31]
            gf_t = gfx_sb
            tmp_t = tmpx_sb
            for ts in range(16):
                t_ac = sch.dma("scalar", lambda e, ts=ts: e.dma_start(
                    out=acol_sb[:, ts % 2:ts % 2 + 1],
                    in_=A_row[0:1, ts * 128:(ts + 1) * 128]),
                    waits=[("dS", t_A)] + ww(("acol", ts % 2)), sem="dS")
                si, sl, sw = slots.acquire()
                for k in range(8):
                    sch.add("tensor", lambda e, sl=sl, ts=ts, k=k: e.matmul(
                        sl[:, 0:16], Cf(k)[:, ts * 128:(ts + 1) * 128],
                        wgs_sb[:, k * 16:(k + 1) * 16], start=(k == 0), stop=False),
                        waits=(sw + [("dS", t_A)] if k == 0 else []), sem=None)
                sch.add("tensor", lambda e, sl=sl, ts=ts: e.matmul(
                    sl[:, 0:16], m_row[0:1, ts * 128:(ts + 1) * 128],
                    negc_sb[0:1, :], start=False, stop=True), sem="pe")
                t_lg = tok("pe")
                sch.add("vector", lambda e, sl=sl, ts=ts: e.tensor_scalar(
                    out=lg, in0=sl[:, 0:16], scalar1=acol_sb[:, ts % 2:ts % 2 + 1],
                    scalar2=None, op0=ALU.mult),
                    waits=[("pe", t_lg), ("dS", t_ac)], sem="dv")
                slots.release(si, "dv", tok("dv"))
                ws(("acol", ts % 2), "dv", tok("dv"))
                sch.add("vector", lambda e: e.tensor_tensor(
                    out=lg, in0=lg, in1=bgbc_sb[:, :], op=ALU.add),
                    waits=dvw(), sem="dv")
                sch.add("vector", lambda e: e.max(out=m8, in_=lg),
                        waits=dvw(), sem="dv")
                sch.add("vector", lambda e: e.tensor_scalar(
                    out=e4, in0=m8[:, 0:4], scalar1=m8[:, 0:1], scalar2=None,
                    op0=ALU.subtract), waits=dvw(), sem="dv")
                t_e4 = tok("dv")
                sch.add("scalar", lambda e: e.activation(
                    out=e4x, in_=e4, func=AF.Exp, accum_out=scs),
                    waits=[("dv", t_e4)], sem="ac")
                t_exp = tok("ac")
                sch.add("vector", lambda e: e.reciprocal(out=rcs, in_=scs),
                        waits=[("ac", t_exp)], sem="dv")
                sch.add("vector", lambda e: e.tensor_scalar(
                    out=e4x, in0=e4x, scalar1=rcs[:, 0:1], scalar2=GSCALE,
                    op0=ALU.mult, op1=ALU.mult), waits=dvw(), sem="dv")
                for i in range(4):
                    dst_t = gf_t if i == 0 else tmp_t
                    sch.add("vector", lambda e, i=i, dst_t=dst_t: e.tensor_scalar(
                        out=dst_t[:, :], in0=lg, scalar1=m8[:, i:i + 1],
                        scalar2=e4x[:, i:i + 1], op0=ALU.is_equal, op1=ALU.mult),
                        waits=dvw(), sem="dv")
                    if i > 0:
                        sch.add("vector", lambda e: e.tensor_tensor(
                            out=gf_t[:, :], in0=gf_t[:, :], in1=tmp_t[:, :], op=ALU.add),
                            waits=dvw(), sem="dv")
                sch.add("vector", lambda e: e.tensor_copy(out=gfb_sb[:, :], in_=gf_t[:, :]),
                        waits=dvw(), sem="dv")
                t_gfb = tok("dv")
                si, sl, sw = slots.acquire()
                slbf = ps_bf[id(sl)]
                sch.add("tensor", lambda e, slbf=slbf: e.transpose(
                    out=slbf[0:16, 0:128], in_=gfb_sb[:, :], identity=ident_sb[:, :]),
                    waits=sw + [("dv", t_gfb), ("dW", t_dw0)], sem="pe")
                t_tp = tok("pe")
                sch.add("vector", lambda e, slbf=slbf, ts=ts: e.tensor_copy(
                    out=gfm_sb[:, ts * 128:(ts + 1) * 128], in_=slbf[0:16, 0:128]),
                    waits=[("pe", t_tp)], sem="dv")
                slots.release(si, "dv", tok("dv"))
            t_gfm = tok("dv")
            t_gate_pe = tok("pe")

            t_MA = build_MA_tiles(t_A)
            t_xn = affine_xn(t_MA, t_gate_pe, ffn_mode=True)

            t_hr = sch.dma("scalar", lambda e: e.dma_start(out=A_row, in_=rt("hr")),
                           waits=[("pe", tok("pe")), ("dS", tok("dS")),
                                  ("dv", t_xn)], sem="dS")
            si, sl, sw = slots.acquire()
            for tb in range(4):
                sch.add("tensor", lambda e, sl=sl, tb=tb: e.matmul(
                    sl[:, tb * 512:(tb + 1) * 512], ones_rf[:, :],
                    A_row[0:1, tb * 512:(tb + 1) * 512], start=True, stop=True),
                    waits=(sw + [("dS", t_hr)] if tb == 0 else []),
                    sem="pe" if tb == 3 else None)
            t_hrmm = tok("pe")
            ws(("Arow",), "pe", t_hrmm)
            ws(("rt",), "dS", t_hr)
            sch.add("vector", lambda e, sl=sl: e.tensor_copy(out=Ab_t, in_=sl[:, :]),
                    waits=[("pe", t_hrmm), ("dv", t_xn)], sem="dv")
            t_hrbc = tok("dv")
            slots.release(si, "dv", t_hrbc)
            for c in range(NCH):
                sch.add("vector", lambda e, c=c: e.tensor_tensor(
                    out=Cf(c), in0=Cf(c), in1=Ab_t, op=ALU.mult), sem="dv")
                ws(("C", c), "dv", tok("dv"))
            t_xf = tok("dv")
            ws(("ATTh", 1), "dv", t_xf)
            if dbg:
                sch.dma("sync", lambda e: e.dma_start(out=DXN, in_=A[:, :]),
                        waits=[("dv", t_xn)])
                sch.dma("sync", lambda e: e.dma_start(out=DGFM, in_=gfm_sb[:, :]),
                        waits=[("dv", t_gfm)])
                sch.dma("sync", lambda e: e.dma_start(
                    out=DROW[0:1, 0:S], in_=m_row), waits=[("dS", t_A)])
                sch.dma("sync", lambda e: e.dma_start(
                    out=DROW[0:1, S:2 * S], in_=A_row), waits=[("dS", t_A)])
                sch.dma("sync", lambda e: e.dma_start(
                    out=DRT, in_=rt_sb[:, :]), waits=[("dS", t_A)])

            w1_tok = {}
            w2_tok = {}
            w1_issued = [0]
            w2_issued = [0]

            def issue_w1(upto):
                while w1_issued[0] < min(upto, E * 8):
                    i = w1_issued[0]
                    e2, pr2 = divmod(i, 8)
                    wts = [w1_slot_tok[i % 4]] if (i % 4) in w1_slot_tok else []
                    if i < 4:
                        wts = wts + ww(("Wpe",)) + ww(("Wac",))
                    w1_tok[(e2, pr2)] = sch.dma(
                        "gpsimd", lambda e, e2=e2, pr2=pr2: e.dma_start(
                            out=w1slot(e2 * 8 + pr2)[:, :].rearrange(
                                "p (k j) -> p k j", k=8),
                            in_=W1P[e2, pr2].rearrange("(k p) j -> p k j", p=128)),
                        waits=wts, sem="dW")
                    w1_issued[0] += 1

            def issue_w2(upto):
                while w2_issued[0] < min(upto, E * 8):
                    i = w2_issued[0]
                    e2, O2 = divmod(i, 8)
                    wts = [w2_slot_tok[i % 3]] if (i % 3) in w2_slot_tok else []
                    if i < 3:
                        wts = wts + ww(("Wpe",)) + ww(("Wac",))
                    w2_tok[(e2, O2)] = sch.dma(
                        "gpsimd", lambda e, e2=e2, O2=O2: e.dma_start(
                            out=w2slot(e2 * 8 + O2)[:, :].rearrange(
                                "p (k j) -> p k j", k=8),
                            in_=W2P[e2, O2].rearrange("(k p) j -> p k j", p=128)),
                        waits=wts, sem="dW")
                    w2_issued[0] += 1

            issue_w1(4)
            issue_w2(3)
            sp_gtok = [0] * 8
            t_chunk_tok = [0] * 8
            for e_ in range(E):
                g0 = e_ * 8
                t_b1 = sch.dma("sync", lambda e, e_=e_: e.dma_start(
                    out=b1_sb[:, (e_ % 2) * 16:(e_ % 2) * 16 + 16], in_=B1H[e_]),
                    waits=ww(("b1", e_ % 2)))
                sch.add("vector", lambda e, e_=e_: e.tensor_scalar(
                    out=b1p_sb[:, (e_ % 2) * 8:(e_ % 2) * 8 + 8].rearrange(
                        "p (a b) -> p a b", b=1),
                    in0=b1_sb[:, (e_ % 2) * 16:(e_ % 2) * 16 + 16].rearrange(
                        "p (a b) -> p a b", b=2)[:, :, 1:2],
                    scalar1=1.0, scalar2=None, op0=ALU.add),
                    waits=[("dX", t_b1)], sem="dv")
                t_b1p = tok("dv")

                grow_w = [("dv", max(t_gfm, t_xf)), ("pe", t_hrmm)] + ww(("grow",))
                if e_ == 0:
                    grow_w.append(("dS", tok("dS")))
                t_gr = sch.dma("scalar", lambda e, e_=e_: e.dma_start(
                    out=bcsrc_sb[0:1, :], in_=gfm_sb[e_:e_ + 1, :]),
                    waits=grow_w, sem="dS")
                si, sl, sw = slots.acquire()
                for tb in range(4):
                    sch.add("tensor", lambda e, sl=sl, tb=tb: e.matmul(
                        sl[:, tb * 512:(tb + 1) * 512], row0_sb[:, :],
                        bcsrc_sb[:, tb * 512:(tb + 1) * 512], start=True, stop=True),
                        waits=(sw + [("dS", t_gr)] if tb == 0 else []),
                        sem="pe" if tb == 3 else None)
                t_gmm = tok("pe")
                ws(("grow",), "pe", t_gmm)
                sch.add("vector", lambda e, sl=sl, e_=e_: e.tensor_scalar(
                    out=gtile(e_), in0=sl[:, :], scalar1=1.0, scalar2=None, op0=ALU.mult),
                    waits=[("pe", t_gmm)] + ww(("ATTh", e_ % 2)), sem="dv")
                t_gt = tok("dv")
                slots.release(si, "dv", t_gt)

                t_silu = 0
                for pr in range(8):
                    for s_ in range(2):
                        si, sl, sw = slots.acquire()
                        for dc in range(NCH):
                            wsl = w1slot(g0 + pr)[:, dc * 256 + s_ * 128:
                                                  dc * 256 + s_ * 128 + 128]
                            for tb in range(4):
                                w0 = (sw + [("dW", w1_tok[(e_, pr)]), ("dv", t_xn)]
                                      if (dc == 0 and tb == 0) else [])
                                sch.add("tensor",
                                        lambda e, sl=sl, wsl=wsl, dc=dc, tb=tb: e.matmul(
                                            sl[:, tb * 512:(tb + 1) * 512], wsl,
                                            Af(dc)[:, tb * 512:(tb + 1) * 512],
                                            start=(dc == 0), stop=(dc == NCH - 1)),
                                        waits=w0,
                                        sem="pe" if (dc == NCH - 1 and tb == 3) else None)
                        t_unit = tok("pe")
                        if s_ == 1:
                            w1_slot_tok[(g0 + pr) % 4] = ("pe", t_unit)
                            issue_w1(g0 + pr + 5)
                        if s_ == 0:
                            t_a = 0
                            silu_toks = []
                            for tb in range(4):
                                w_ = [("pe", t_unit), ("dX", t_b1)]
                                if tb >= 2:
                                    w_.append(("ac", silu_toks[tb - 2]))
                                sch.add("vector",
                                        lambda e, sl=sl, tb=tb, pr=pr, e_=e_: e.tensor_scalar(
                                            out=T1(tb), in0=sl[:, tb * 512:(tb + 1) * 512],
                                            scalar1=b1_sb[:, (e_ % 2) * 16 + pr * 2:
                                                          (e_ % 2) * 16 + pr * 2 + 1],
                                            scalar2=7.0, op0=ALU.add, op1=ALU.min),
                                        waits=w_, sem="dv")
                                t_a = tok("dv")
                                sch.add("scalar", lambda e, tb=tb: e.activation(
                                    out=T2[:, tb * 512:(tb + 1) * 512], in_=T1(tb),
                                    func=AF.Silu, scale=1.702),
                                    waits=[("dv", t_a)], sem="ac")
                                silu_toks.append(tok("ac"))
                            t_silu = tok("ac")
                            slots.release(si, "dv", t_a)
                        else:
                            def emit_bts(tb):
                                sch.add("vector",
                                        lambda e, sl=sl, tb=tb, pr=pr, e_=e_: e.tensor_scalar(
                                            out=T1(tb), in0=sl[:, tb * 512:(tb + 1) * 512],
                                            scalar1=b1p_sb[:, (e_ % 2) * 8 + pr:
                                                           (e_ % 2) * 8 + pr + 1],
                                            scalar2=-6.0, op0=ALU.add, op1=ALU.max),
                                        waits=[("pe", t_unit), ("dv", t_b1p),
                                               ("ac", t_silu)], sem="dv")
                                return tok("dv")

                            def emit_sg(tb):
                                sch.add("vector", lambda e, tb=tb: e.scalar_tensor_tensor(
                                    out=T1(tb), in0=T1(tb), scalar=8.0,
                                    in1=T2[:, tb * 512:(tb + 1) * 512],
                                    op0=ALU.min, op1=ALU.mult), sem="dv")
                                w1_ = ([("dv", t_gt)] + ww(("sp", pr)) if tb == 0 else [])
                                sch.add("vector",
                                        lambda e, tb=tb, pr=pr, e_=e_: e.tensor_tensor(
                                            out=spr(pr)[:, tb * 512:(tb + 1) * 512],
                                            in0=T1(tb),
                                            in1=gtile(e_)[:, tb * 512:(tb + 1) * 512],
                                            op=ALU.mult),
                                        waits=w1_, sem="dv")
                            emit_bts(0)
                            emit_bts(1)
                            emit_sg(0)
                            emit_bts(2)
                            emit_sg(1)
                            t_bts3 = emit_bts(3)
                            emit_sg(2)
                            emit_sg(3)
                            sp_gtok[pr] = tok("dv")
                            slots.release(si, "dv", t_bts3)
                if dbg and e_ == 0:
                    sch.dma("sync", lambda e: e.dma_start(
                        out=DSP, in_=Bv[:, 0:16384]),
                        waits=[("dv", tok("dv"))])
                ws(("ATTh", e_ % 2), "dv", tok("dv"))
                ws(("b1", e_ % 2), "dv", tok("dv"))

                for O in range(8):
                    si, sl, sw = slots.acquire()
                    for ic in range(NCH):
                        wsl = w2slot(g0 + O)[:, ic * 128:(ic + 1) * 128]
                        stop = (ic == NCH - 1) and (e_ != E - 1)
                        for tb in range(4):
                            w0 = (sw + [("dW", w2_tok[(e_, O)]), ("dv", sp_gtok[ic])]
                                  if (ic == 0 and tb == 0) else
                                  ([("dv", sp_gtok[ic])] if tb == 0 else []))
                            last_mm = (ic == NCH - 1 and tb == 3 and e_ != E - 1)
                            sch.add("tensor",
                                    lambda e, sl=sl, wsl=wsl, ic=ic, tb=tb, stop=stop: e.matmul(
                                        sl[:, tb * 512:(tb + 1) * 512], wsl,
                                        spr(ic)[:, tb * 512:(tb + 1) * 512],
                                        start=(ic == 0), stop=stop),
                                    waits=w0, sem="pe" if last_mm else None)
                    if e_ == E - 1:
                        for tb in range(4):
                            w0 = ([("dv", max(t_gfm, t_b2v))] if tb == 0 else [])
                            sch.add("tensor", lambda e, sl=sl, O=O, tb=tb: e.matmul(
                                sl[:, tb * 512:(tb + 1) * 512],
                                b2v_sb[:, O * 128:(O + 1) * 128],
                                gfm_sb[:, tb * 512:(tb + 1) * 512],
                                start=False, stop=True),
                                waits=w0, sem="pe" if tb == 3 else None)
                    t_yu = tok("pe")
                    w2_slot_tok[(g0 + O) % 3] = ("pe", t_yu)
                    issue_w2(g0 + O + 4)
                    w_dr = [("pe", t_yu)] + (ww(("C", O)) if e_ == 0 else [])
                    sch.add("vector", lambda e, sl=sl, O=O: e.tensor_tensor(
                        out=Cf(O), in0=sl[:, :], in1=Cf(O), op=ALU.add),
                        waits=w_dr, sem="dv")
                    t_chunk_tok[O] = tok("dv")
                    slots.release(si, "dv", t_chunk_tok[O])
                t_w2pe = tok("pe")
                for pr in range(8):
                    ws(("sp", pr), "pe", t_w2pe)

            if dbg:
                for c in range(NCH):
                    sch.dma("sync", lambda e, c=c: e.dma_start(
                        out=DT[:, c * S:(c + 1) * S], in_=Cf(c)),
                        waits=[("dv", t_chunk_tok[c])])
            for c in range(NCH):
                stg = ATTh(c % 2)
                t_in = sch.dma("sync", lambda e, c=c, stg=stg: e.dma_start(
                    out=stg, in_=src[bi, c * 128:(c + 1) * 128, :]),
                    waits=ww(("ATTh", c % 2)))
                sch.add("vector", lambda e, c=c, stg=stg: e.tensor_tensor(
                    out=stg, in0=stg, in1=Cf(c), op=ALU.add),
                    waits=[("dX", t_in), ("dv", t_chunk_tok[c])], sem="dv")
                t_add = tok("dv")
                t_out = sch.dma("sync", lambda e, c=c, stg=stg, dst=dst: e.dma_start(
                    out=dst[bi, c * 128:(c + 1) * 128, :], in_=stg),
                    waits=[("dv", t_add)], sem=out_sem)
                ws(("ATTh", c % 2), out_sem, t_out)
                ws(("C", c), "dv", t_add)
            t_end_pe = tok("pe")
            for c in range(NCH):
                ws(("A", c), "pe", t_end_pe)
            ws(("B",), "pe", t_end_pe)
            ws(("Wdv",), "dv", tok("dv"))
            ws(("Wac",), "ac", tok("ac"))
            ws(("Wpe",), "pe", t_end_pe)

        def attn_call(src, dst, bi, dbg=False):
            x_toks, t_A = ln_prologue(src, bi, ffn_mode=False)
            t_MA = build_MA_tiles(t_A)
            t_xln = affine_xn(t_MA, None, ffn_mode=False)
            ws(("xstg", 0), "dv", t_xln)
            ws(("xstg", 1), "dv", t_xln)

            wv_toks = []
            for dc in range(NCH):
                t = sch.dma("gpsimd", lambda e, dc=dc: e.dma_start(
                    out=wv_sl(dc), in_=WQKV[dc * 128:(dc + 1) * 128, 2 * D:3 * D]),
                    waits=ww(("w1r", dc // 2)) + (ww(("Wac",)) if dc == 0 else []),
                    sem="dW")
                wv_toks.append(t)
            sch.add("vector", lambda e: e.memset(
                Bv[:, 0:16640].rearrange("p (g c) -> p g c", c=65)[:, :, 64:65], 1.0),
                waits=ww(("B",)) + ww(("sp", 0)) + ww(("sp", 7)), sem="dv")
            t_vones = tok("dv")
            v_toks = []
            for tc in range(16):
                si, sl, sw = slots.acquire()
                for dc in range(NCH):
                    for nb in range(2):
                        w0 = (sw + [("dW", wv_toks[7]), ("dv", t_xln)]
                              if (dc == 0 and nb == 0) else [])
                        sch.add("tensor", lambda e, sl=sl, dc=dc, nb=nb, tc=tc: e.matmul(
                            sl[:, nb * 512:(nb + 1) * 512],
                            Af(dc)[:, tc * 128:(tc + 1) * 128],
                            wv_sl(dc)[:, nb * 512:(nb + 1) * 512],
                            start=(dc == 0), stop=(dc == NCH - 1)),
                            waits=w0,
                            sem="pe" if (dc == NCH - 1 and nb == 1) else None)
                t_vu = tok("pe")
                sch.add("vector", lambda e, sl=sl, tc=tc: e.tensor_tensor(
                    out=Bv[:, tc * 1040:(tc + 1) * 1040].rearrange(
                        "p (h c) -> p h c", c=65)[:, :, 0:64],
                    in0=sl[:, 0:1024].rearrange("p (h c) -> p h c", c=64),
                    in1=bvt_sb[:, :].rearrange("p (h c) -> p h c", c=64), op=ALU.add),
                    waits=[("pe", t_vu), ("dv", max(t_vones, t_b2v))], sem="dv")
                v_toks.append(tok("dv"))
                slots.release(si, "dv", v_toks[-1])
            t_vall = tok("dv")
            if dbg:
                sch.dma("sync", lambda e: e.dma_start(out=DXL, in_=A[:, :]),
                        waits=[("dv", t_xln)])
                sch.dma("sync", lambda e: e.dma_start(out=DVT, in_=Bv[:, :]),
                        waits=[("dv", t_vall)])

            ctx_tok = [0] * NCH
            ctx_sem = ["dv"] * NCH
            for h in range(NH):
                hc = h // 2
                t_wq = sch.dma("gpsimd", lambda e, h=h: e.dma_start(
                    out=qkw(0)[:, :].rearrange("p (k m) -> p k m", k=8),
                    in_=QKP[:, h * 128:(h + 1) * 128].rearrange(
                        "(k p) m -> p k m", p=128)),
                    waits=ww(("qkw", 0)) + (ww(("Wdv",)) + ww(("Wac",))
                          if h == 0 else []), sem="dW")
                t_wk = sch.dma("gpsimd", lambda e, h=h: e.dma_start(
                    out=qkw(1)[:, :].rearrange("p (k m) -> p k m", k=8),
                    in_=QKP[:, 2048 + h * 128:2048 + (h + 1) * 128].rearrange(
                        "(k p) m -> p k m", p=128)),
                    waits=ww(("qkw", 1)) + (ww(("Wdv",)) + ww(("Wac",))
                          if h == 0 else []), sem="dW")
                si, sl, sw = slots.acquire()
                for dc in range(NCH):
                    for tb in range(4):
                        w0 = (sw + [("dW", t_wq), ("dv", t_xln)]
                              if (dc == 0 and tb == 0) else [])
                        sch.add("tensor", lambda e, sl=sl, dc=dc, tb=tb: e.matmul(
                            sl[:, tb * 512:(tb + 1) * 512],
                            qkw(0)[:, dc * 128:(dc + 1) * 128],
                            Af(dc)[:, tb * 512:(tb + 1) * 512],
                            start=(dc == 0), stop=(dc == NCH - 1)),
                            waits=w0,
                            sem="pe" if (dc == NCH - 1 and tb == 3) else None)
                t_qmm = tok("pe")
                ws(("qkw", 0), "pe", t_qmm)
                sch.add("vector", lambda e, sl=sl, h=h: e.tensor_scalar(
                    out=q_tile[:, :], in0=sl[:, :], scalar1=bq_sb[:, h:h + 1],
                    scalar2=None, op0=ALU.add),
                    waits=[("pe", t_qmm)] + ww(("qt",)) + ww(("w2r", 0)) +
                          ww(("w2r", 1)) + ww(("Wdv",)), sem="dv")
                t_q = tok("dv")
                slots.release(si, "dv", t_q)
                si, sl, sw = slots.acquire()
                for dc in range(NCH):
                    for tb in range(4):
                        w0 = (sw + [("dW", t_wk)] if (dc == 0 and tb == 0) else [])
                        sch.add("tensor", lambda e, sl=sl, dc=dc, tb=tb: e.matmul(
                            sl[:, tb * 512:(tb + 1) * 512],
                            qkw(1)[:, dc * 128:(dc + 1) * 128],
                            Af(dc)[:, tb * 512:(tb + 1) * 512],
                            start=(dc == 0), stop=(dc == NCH - 1)),
                            waits=w0,
                            sem="pe" if (dc == NCH - 1 and tb == 3) else None)
                t_kmm = tok("pe")
                ws(("qkw", 1), "pe", t_kmm)
                sch.add("vector", lambda e, sl=sl, h=h: e.tensor_scalar(
                    out=k_tile[:, :], in0=sl[:, :], scalar1=bq_sb[:, 16 + h:16 + h + 1],
                    scalar2=None, op0=ALU.add),
                    waits=[("pe", t_kmm)] + ww(("kt",)) + ww(("w2r", 2)) +
                          ww(("Wdv",)), sem="dv")
                t_k = tok("dv")
                slots.release(si, "dv", t_k)

                for hh in range(1):
                    first_bias = (h == 0)
                    t_bs = sch.dma("sync", lambda e, h=h: e.dma_start(
                        out=bsh_stage, in_=BSH[h]),
                        waits=ww(("bst",)) + (ww(("w1r", 0)) + ww(("w1r", 1)) +
                              ww(("Wdv",)) + ww(("Wac",)) +
                              [("pe", tok("pe"))] if first_bias else []))
                    sch.add("scalar", lambda e: e.activation(
                        out=ebias_b, in_=bsh_stage, func=AF.Exp),
                        waits=[("dX", t_bs)] + ww(("ATTh", 0)) + ww(("ATTh", 1)) +
                              ww(("ebias",)),
                        sem="ac")
                    t_eb = tok("ac")
                    if dbg and h == 0:
                        sch.dma("sync", lambda e: e.dma_start(out=DEB, in_=ebias_t),
                                waits=[("ac", t_eb)])
                    ws(("bst",), "ac", t_eb)
                    war.pop(("ATTh", 0), None)
                    war.pop(("ATTh", 1), None)

                    ctx_si, ctx_sl, ctx_sw = slots.acquire()
                    sc_si, sc_sl, sc_sw = slots.acquire()
                    exp_tok = [[0, 0] for _ in range(NKT)]
                    es_tok = [0] * NKT

                    def exptmp4(idx):
                        return Cb[:, 8192 + idx * 1024:8192 + (idx + 1) * 1024]

                    def emit_scores_half(kt, qh):
                        for tb2 in range(2):
                            q0 = qh * 1024 + tb2 * 512
                            w0 = []
                            if tb2 == 0:
                                w0 = [("dv", max(t_q, t_k))]
                                w0 += (sc_sw if kt == 0 and qh == 0 else [])
                                if kt > 0:
                                    w0.append(("ac", exp_tok[kt - 1][qh]))
                            sch.add("tensor",
                                    lambda e, sc_sl=sc_sl, q0=q0, qh=qh, kt=kt, tb2=tb2: e.matmul(
                                        sc_sl[:, qh * 1024 + tb2 * 512:
                                              qh * 1024 + tb2 * 512 + 512],
                                        k_tile[:, kt * 128:(kt + 1) * 128],
                                        q_tile[:, q0:q0 + 512],
                                        start=True, stop=True),
                                    waits=w0, sem="pe" if tb2 == 1 else None)
                        t_sc = tok("pe")
                        idx = (kt % 2) * 2 + qh
                        sch.add("scalar", lambda e, sc_sl=sc_sl, kt=kt, qh=qh, idx=idx: e.activation(
                            out=exptmp4(idx), in_=sc_sl[:, qh * 1024:(qh + 1) * 1024],
                            func=AF.Exp, scale=0.125),
                            waits=[("pe", t_sc), ("ac", t_eb)] + ww(("exptmp", idx)),
                            sem="ac")
                        exp_tok[kt][qh] = tok("ac")
                        c0 = 1920 - kt * 128 + qh * 1024
                        sch.add("vector", lambda e, kt=kt, qh=qh, c0=c0, idx=idx: e.tensor_tensor(
                            out=expS(kt)[:, qh * 1024:(qh + 1) * 1024], in0=exptmp4(idx),
                            in1=ebias_b[:, c0:c0 + 1024], op=ALU.mult),
                            waits=[("ac", exp_tok[kt][qh])] +
                                  (ww(("expS", kt % 2)) if qh == 0 else []), sem="dv")
                        es_tok[kt] = tok("dv")
                        ws(("exptmp", idx), "dv", es_tok[kt])

                    def emit_ctx(kt):
                        for tb in range(4):
                            w0 = []
                            if tb == 0:
                                w0 = [("dv", es_tok[kt]), ("dv", t_vall)]
                                w0 += (ctx_sw if kt == 0 else [])
                            sch.add("tensor",
                                    lambda e, ctx_sl=ctx_sl, tb=tb, kt=kt, h=h: e.matmul(
                                        ctx_sl[0:65, tb * 512:(tb + 1) * 512],
                                        vsl(kt, h),
                                        expS(kt)[:, tb * 512:(tb + 1) * 512],
                                        start=(kt == 0), stop=(kt == NKT - 1)),
                                    waits=w0, sem="pe" if tb == 3 else None)
                        ws(("expS", kt % 2), "pe", tok("pe"))

                    emit_scores_half(0, 0)
                    emit_scores_half(0, 1)
                    for kt in range(1, NKT):
                        emit_scores_half(kt, 0)
                        emit_scores_half(kt, 1)
                        emit_ctx(kt - 1)
                    emit_ctx(NKT - 1)
                    t_ctx = tok("pe")
                    ws(("ebias",), "dv", tok("dv"))
                    sch.add("vector", lambda e, ctx_sl=ctx_sl: e.reciprocal(
                        out=exptmp(0)[64:65, :], in_=ctx_sl[64:65, :]),
                        waits=[("pe", t_ctx)] + ww(("exptmp", 0)), sem="dv")
                    t_rc = tok("dv")
                    sch.add("vector", lambda e: e.tensor_copy(
                        out=exptmp(0)[64:65, :].bitcast(BF16)[:, 0:2048],
                        in_=exptmp(0)[64:65, :]), waits=[("dv", t_rc)], sem="dv")
                    t_rcb16 = tok("dv")
                    t_rr2 = sch.dma("scalar", lambda e: e.dma_start(
                        out=bcsrc_sb[0:1, :],
                        in_=exptmp(0)[64:65, :].bitcast(BF16)[:, 0:2048]),
                        waits=[("dv", t_rcb16)] + ww(("rrow",)) + ww(("grow",)),
                        sem="dS")
                    for tb in range(4):
                        w0 = ([("dS", t_rr2), ("ac", tok("ac")),
                               ("pe", tok("pe"))] if tb == 0 else [])
                        sch.add("tensor", lambda e, sc_sl=sc_sl, tb=tb: e.matmul(
                            sc_sl[0:64, tb * 512:(tb + 1) * 512], row0_sb[:, 0:64],
                            bcsrc_sb[:, tb * 512:(tb + 1) * 512], start=True, stop=True),
                            waits=w0, sem="pe" if tb == 3 else None)
                    t_rbc = tok("pe")
                    ws(("rrow",), "pe", t_rbc)
                    ws(("grow",), "pe", t_rbc)
                    sch.add("vector", lambda e, sc_sl=sc_sl: e.tensor_copy(
                        out=exptmp(1)[0:64, :], in_=sc_sl[0:64, :]),
                        waits=[("pe", t_rbc)] + ww(("exptmp", 1)), sem="dv")
                    t_rcb = tok("dv")
                    ws(("exptmp", 0), "dv", t_rcb)
                    ws(("exptmp", 1), "dv", t_rcb)
                    if dbg and h == 0:
                        sch.dma("sync", lambda e: e.dma_start(
                            out=DNR[0:1, :], in_=exptmp(0)[64:65, :]),
                            waits=[("dv", t_rc)])
                        sch.dma("sync", lambda e: e.dma_start(
                            out=DNR[1:2, :], in_=rrow_sb), waits=[("dS", t_rr2)])
                        sch.dma("sync", lambda e: e.dma_start(
                            out=DNR[2:3, :], in_=exptmp(1)[0:1, :]),
                            waits=[("dv", t_rcb)])
                        sch.dma("sync", lambda e: e.dma_start(
                            out=DES, in_=expS(15)), waits=[("dv", es_tok[15])])
                    if h % 2 == 0:
                        sch.add("vector", lambda e, ctx_sl=ctx_sl, hc=hc: e.tensor_tensor(
                            out=ctxf(hc)[0:64, :], in0=ctx_sl[0:64, :],
                            in1=exptmp(1)[0:64, :], op=ALU.mult),
                            waits=dvw(), sem="dv")
                        t_cw = tok("dv")
                        ctx_tok[hc] = t_cw
                        ctx_sem[hc] = "dv"
                    else:
                        sch.add("vector", lambda e, ctx_sl=ctx_sl: e.tensor_tensor(
                            out=expS(0)[0:64, :], in0=ctx_sl[0:64, :],
                            in1=exptmp(1)[0:64, :], op=ALU.mult),
                            waits=ww(("expS", 0)) + dvw(), sem="dv")
                        t_cw = tok("dv")
                        ws(("expS", 0), "dv", t_cw)
                        t_cd = sch.dma("scalar", lambda e, hc=hc: e.dma_start(
                            out=ctxf(hc)[64:128, :], in_=expS(0)[0:64, :]),
                            waits=[("dv", t_cw)], sem="dS")
                        ctx_tok[hc] = t_cd
                        ctx_sem[hc] = "dS"
                        ws(("expS", 0), "dS", t_cd)
                    slots.release(ctx_si, "dv", t_cw)
                    slots.release(sc_si, "dv", t_rcb)
                t_hcpe = tok("pe")
                ws(("qt",), "pe", t_hcpe)
                ws(("kt",), "pe", t_hcpe)
                for pr_ in range(8):
                    ws(("sp", pr_), "pe", t_hcpe)
                ws(("B",), "pe", t_hcpe)

            if dbg:
                sch.dma("sync", lambda e: e.dma_start(
                    out=DCTX, in_=Cb[:, 16384:32768]),
                    waits=[(ctx_sem[j], ctx_tok[j]) for j in range(NCH)])
            for oc in range(NCH):
                w_wd = ww(("wdw", oc % 2))
                if oc < 2:
                    w_wd = w_wd + [("pe", tok("pe"))]
                t_wd = sch.dma("gpsimd", lambda e, oc=oc: e.dma_start(
                    out=wdw(oc)[:, :].rearrange("p (k m) -> p k m", k=8),
                    in_=WD[:, oc * 128:(oc + 1) * 128].rearrange(
                        "(k p) m -> p k m", p=128)),
                    waits=w_wd, sem="dW")
                si, sl, sw = slots.acquire()
                for ic in range(NCH):
                    for tb in range(4):
                        w0 = []
                        if ic == 0 and tb == 0:
                            w0 = sw + [("dW", t_wd)]
                            w0 += [(ctx_sem[j], ctx_tok[j]) for j in range(NCH)]
                        sch.add("tensor", lambda e, sl=sl, oc=oc, ic=ic, tb=tb: e.matmul(
                            sl[:, tb * 512:(tb + 1) * 512],
                            wdw(oc)[:, ic * 128:(ic + 1) * 128],
                            ctxf(ic)[:, tb * 512:(tb + 1) * 512],
                            start=(ic == 0), stop=(ic == NCH - 1)),
                            waits=w0,
                            sem="pe" if (ic == NCH - 1 and tb == 3) else None)
                t_dmm = tok("pe")
                ws(("wdw", oc % 2), "pe", t_dmm)
                stg = xstg(oc % 2)
                w_stg = ww(("xstg", oc % 2))
                if oc < 2:
                    w_stg = w_stg + [("pe", tok("pe")), ("dS", tok("dS")),
                                     ("dv", tok("dv"))]
                t_in = sch.dma("sync", lambda e, oc=oc, stg=stg, src=src, bi=bi: e.dma_start(
                    out=stg, in_=src[bi, oc * 128:(oc + 1) * 128, :]),
                    waits=w_stg)
                sch.add("vector", lambda e, sl=sl, oc=oc, stg=stg: e.scalar_tensor_tensor(
                    out=stg, in0=sl[:, :], scalar=bd_sb[:, oc:oc + 1], in1=stg,
                    op0=ALU.add, op1=ALU.add),
                    waits=[("pe", t_dmm), ("dX", t_in)], sem="dv")
                t_add = tok("dv")
                slots.release(si, "dv", t_add)
                t_out = sch.dma("sync", lambda e, oc=oc, stg=stg, dst=dst, bi=bi: e.dma_start(
                    out=dst[bi, oc * 128:(oc + 1) * 128, :], in_=stg),
                    waits=[("dv", t_add)])
                ws(("xstg", oc % 2), "dX", t_out)
            t_end_pe = tok("pe")
            for c in range(NCH):
                ws(("C", c), "pe", t_end_pe)
                ws(("A", c), "pe", t_end_pe)
            ws(("ATTh", 0), "dv", tok("dv"))
            ws(("ATTh", 1), "dv", tok("dv"))
            ws(("Wpe",), "pe", t_end_pe)
            ws(("Wac",), "ac", tok("ac"))
            ws(("mrow",), "pe", t_end_pe)
            ws(("Arow",), "pe", t_end_pe)
            ws(("rt",), "dS", tok("dS"))

        for bi in range(BPC):
            ffn_call(XIN, X1, bi, "dX", dbg=(DBG and bi == 0))
        sch.barrier()
        war.clear()
        for bi in range(BPC):
            attn_call(X1, X2, bi, dbg=(DBG and bi == 0))
        sch.barrier()
        war.clear()
        for bi in range(BPC):
            ffn_call(X2, OUT, bi, "dO")

        sch.add("gpsimd", lambda e: e.engine_nop(),
                waits=[("dO", sch.cnt["dO"]), ("dX", sch.cnt["dX"]),
                       ("dS", sch.cnt["dS"]), ("dW", sch.cnt["dW"])], sem="gp")

        sch.simulate()

        def mk(eng_name):
            def f(eng):
                seen = {}
                for (en, fn, waits, incs) in sch.ops:
                    if en != eng_name:
                        continue
                    for (s, v) in waits:
                        if seen.get(s, 0) < v:
                            eng.wait_ge(sems[s], v)
                            seen[s] = v
                    if fn is None:
                        continue
                    inst = fn(eng)
                    for (s, n) in incs:
                        inst = inst.then_inc(sems[s], n)
            return f

        block.sync(mk("sync"))
        block.tensor(mk("tensor"))
        block.vector(mk("vector"))
        block.scalar(mk("scalar"))
        block.gpsimd(mk("gpsimd"))
    return nc


def _bucket_vec():
    rel = np.arange(-(S - 1), S, dtype=np.int64)
    nb = NB // 2
    buckets = np.where(rel > 0, nb, 0)
    r = np.abs(rel)
    max_exact = nb // 2
    is_small = r < max_exact
    tmp = np.log(np.maximum(r, 1).astype(np.float32) / max_exact)
    large = tmp / math.log(MAXD / max_exact) * (nb - max_exact)
    large = max_exact + large.astype(np.int32)
    large = np.minimum(large, nb - 1)
    return (buckets + np.where(is_small, r, large)).astype(np.int64)


def kernel(**inputs):
    f32 = np.float32
    x = np.asarray(inputs["L0_attn_hidden_states"], f32)
    w_qkv = np.ascontiguousarray(np.asarray(inputs["L0_attn_w_qkv"], f32))
    b_qkv = np.asarray(inputs["L0_attn_b_qkv"], f32)
    w_dense = np.ascontiguousarray(np.asarray(inputs["L0_attn_w_dense"], f32))
    b_dense = np.asarray(inputs["L0_attn_b_dense"], f32)
    table = np.asarray(inputs["L0_attn_rel_attn_table"], f32)
    norm_scale = np.asarray(inputs["L0_ffn_norm_scale"], f32)
    wg = np.asarray(inputs["L0_ffn_wg"], f32)
    bg = np.asarray(inputs["L0_ffn_bg"], f32)
    w1 = np.asarray(inputs["L0_ffn_w1"], f32)
    b1 = np.asarray(inputs["L0_ffn_b1"], f32)
    w2 = np.asarray(inputs["L0_ffn_w2"], f32)
    b2 = np.asarray(inputs["L0_ffn_b2"], f32)

    W1P = np.ascontiguousarray(
        w1.reshape(E, D, 8, 128, 2).transpose(0, 2, 1, 4, 3).reshape(E, 8, D, 256))
    W2P = np.ascontiguousarray(w2.reshape(E, D, 8, 128).transpose(0, 2, 1, 3))
    B1H = np.ascontiguousarray(
        b1.reshape(E, 8, 128, 2).transpose(0, 2, 1, 3).reshape(E, 128, 16))
    B2H = np.ascontiguousarray(b2)
    QKP = np.zeros((D, 4096), f32)
    BQH = np.zeros((128, 32), f32)
    for h in range(16):
        QKP[:, h * 128:h * 128 + 64] = w_qkv[:, h * 64:(h + 1) * 64]
        QKP[:, 2048 + h * 128:2048 + h * 128 + 64] = w_qkv[:, D + h * 64:D + (h + 1) * 64]
        BQH[0:64, h] = b_qkv[h * 64:(h + 1) * 64]
        BQH[0:64, 16 + h] = b_qkv[D + h * 64:D + (h + 1) * 64]
    QKP = np.ascontiguousarray(QKP)
    BQH = np.ascontiguousarray(BQH)
    BVR = np.ascontiguousarray(b_qkv[2 * D:3 * D].reshape(1, D))
    BDH = np.ascontiguousarray(b_dense.reshape(8, 128).T)
    NSH = np.ascontiguousarray(norm_scale.reshape(8, 128).T)
    WGH = np.ascontiguousarray(wg.reshape(8, 128, 16))
    BGH = np.ascontiguousarray(bg.reshape(1, 16))
    IDE = np.eye(128, dtype=f32)

    bv = _bucket_vec()
    bias_vec = table[:, bv]
    idx = (np.arange(128)[:, None] - np.arange(BSW)[None, :]) + 3967
    BSHh = np.ascontiguousarray(bias_vec[:, idx])

    nc = build_nc()
    shared = dict(W1P=W1P, W2P=W2P, B1H=B1H, B2H=B2H, WQKV=w_qkv, QKP=QKP, BQH=BQH,
                  BVR=BVR, WD=w_dense, BDH=BDH, NSH=NSH, WGH=WGH, BGH=BGH,
                  BSH=BSHh, IDE=IDE)
    in_maps = []
    for c in range(8):
        m = dict(shared)
        m["XIN"] = np.ascontiguousarray(x[c * BPC:(c + 1) * BPC].transpose(0, 2, 1))
        in_maps.append(m)
    res = run_bass_kernel_spmd(nc, in_maps, core_ids=list(range(8)))
    outs = []
    for c in range(8):
        o = res.results[c]["OUT"]
        outs.append(np.ascontiguousarray(np.asarray(o, f32).transpose(0, 2, 1)))
    return np.concatenate(outs, axis=0)


if __name__ == "__main__":
    nc = build_nc()
    print("instructions:", sum(1 for _ in nc.all_instructions()))
